# version 02 (final)
# baseline (speedup 1.0000x reference)
import numpy as np
import ml_dtypes
import concourse.bass as bass
import concourse.mybir as mybir
from concourse.bass_utils import run_bass_kernel_spmd
from contextlib import ExitStack

F32 = mybir.dt.float32
BF16 = mybir.dt.bfloat16
F16 = mybir.dt.float16
AF = mybir.ActivationFunctionType
ALU = mybir.AluOpType
AX = mybir.AxisListType

NCORES = 8
D = 1024
DH = 4096
U = 64
W = 64
TOK = 2048
NT = 18
NTS = NT * 128
MAGIC = 1536.0
NJP = 32
EW = [512, 512, 512, 512, 256]
EO = [0, 512, 1024, 1536, 2048]
TCH = 256
NCH = TOK // TCH
CW = TCH + 2


def build(dbg=False, phase=6, sub=4, rd=False, serial=False):
    nc = bass.Bass("TRN2", target_bir_lowering=False, detect_race_conditions=rd)

    x_in = nc.dram_tensor("x", [NTS, D], F32, kind="ExternalInput").ap()
    wtd_in = nc.dram_tensor("i_wtd", [D, 128], F32, kind="ExternalInput").ap()
    wx_in = nc.dram_tensor("i_wx", [D, U], F32, kind="ExternalInput").ap()
    wabd_in = nc.dram_tensor("i_wabd", [128, 2], F32, kind="ExternalInput").ap()
    bh2_in = nc.dram_tensor("i_bh2", [128, 1], F32, kind="ExternalInput").ap()
    ba_in = nc.dram_tensor("i_ba128", [128, 1], F32, kind="ExternalInput").ap()
    eps_in = nc.dram_tensor("i_eps", [128, 1], F32, kind="ExternalInput").ap()
    zero5_in = nc.dram_tensor("i_zero5", [128, 512], BF16, kind="ExternalInput").ap()
    mask_in = nc.dram_tensor("i_maskattn", [128, 2 * W], F32, kind="ExternalInput").ap()
    maskL_in = nc.dram_tensor("i_maskL", [128, 1], F32, kind="ExternalInput").ap()
    maskL2_in = nc.dram_tensor("i_maskL2", [128, 1], F32, kind="ExternalInput").ap()
    maskR_in = nc.dram_tensor("i_maskR", [128, 1], F32, kind="ExternalInput").ap()
    maskR2_in = nc.dram_tensor("i_maskR2", [128, 1], F32, kind="ExternalInput").ap()
    cw3_in = nc.dram_tensor("i_cw3", [128, 96], F32, kind="ExternalInput").ap()
    cb_in = nc.dram_tensor("i_convb", [128, 32], F32, kind="ExternalInput").ap()
    eyeb_in = nc.dram_tensor("i_eyeb", [128, 128], BF16, kind="ExternalInput").ap()
    eyeh_in = nc.dram_tensor("i_eyeh", [128, 128], F16, kind="ExternalInput").ap()
    eyef_in = nc.dram_tensor("i_eyef", [128, 128], F32, kind="ExternalInput").ap()
    onesf_in = nc.dram_tensor("i_onesf", [128, 128], F32, kind="ExternalInput").ap()
    ones_r_in = nc.dram_tensor("i_ones_r", [1, 128], F32, kind="ExternalInput").ap()
    ones_h_in = nc.dram_tensor("i_ones_h", [128, 1], F16, kind="ExternalInput").ap()
    w1T_in = nc.dram_tensor("w1T", [D, DH], F32, kind="ExternalInput").ap()
    w2T_in = nc.dram_tensor("w2T", [DH, D], F32, kind="ExternalInput").ap()
    y_out = nc.dram_tensor("y", [TOK, D], F32, kind="ExternalOutput").ap()

    zshear = nc.dram_tensor("zshear", [NT * 128 * 256], BF16, kind="Internal")
    w1q_d = nc.dram_tensor("w1q_d", [128, 32 * 8 * 128], F16, kind="Internal")
    w2q_d = nc.dram_tensor("w2q_d", [128, 8 * 32 * 128], F16, kind="Internal")
    x2_d = nc.dram_tensor("x2_d", [TOK, D], F32, kind="Internal")

    dbg_t = {}
    if dbg:
        for nm, shp, dt_ in [("d_xn", [128, D], BF16), ("d_q2", [128, NTS], BF16),
                             ("d_k2", [128, 64 + NTS], BF16), ("d_pp", [128, W], F32),
                             ("d_x2", [128, D], F32), ("d_xq1", [128, D], F16),
                             ("d_h", [128, CW], F32), ("d_cv", [128, TCH], F32),
                             ("d_xq2", [128, TCH], F16), ("d_sw", [128, 4], F32),
                             ("d_bt", [128, 256], BF16), ("d_rz", [128, NT], F32),
                             ("d_s1", [1, NTS], F32), ("d_cs1", [128, 32], F32),
                             ("d_s2bc", [128, TCH], F32),
                             ("d_zt", [18 * 128, 256], BF16),
                             ("d_ppall", [128, NT * W], F32)]:
            dbg_t[nm] = nc.dram_tensor(nm, shp, dt_, kind="ExternalOutput").ap()

    stk = ExitStack()
    with stk:
        e = stk.enter_context

        sem_names = ["x", "c", "cg", "zz", "wA0", "wA1", "wB0", "wB1", "wBw0", "wBw1", "shw0", "shw1", "shr0", "shr1", "w1l0", "w1l1", "xrl0", "xrl1", "yo0", "yo1", "zw", "wA", "wAq", "sw", "wB", "wBq",
                     "wBw", "csA", "cse", "cst", "ln", "lnr", "qkt",
                     "qke", "qkm", "qev", "ksh", "hv", "ht", "he", "ee",
                     "ptr", "pms", "mq", "pex", "pcv", "shw", "shr", "btr",
                     "bev", "bev2", "vmm", "vev", "st2", "tq1", "tqt", "tqe", "s1p",
                     "s1e", "s1d", "spl", "bar", "w2l", "w1l", "m1", "h1",
                     "s1v", "cva", "cvb", "sil", "mxv", "mxt", "mxq", "mxt2",
                     "s2e", "s2p", "s2v", "qml", "xrl", "m2", "e1", "otr",
                     "e2g", "e2", "yo", "dbgs", "sq1s", "sq2s", "sm1", "sm2"]
        SH, SV = {}, {}
        for n in sem_names:
            SH[n] = e(nc.semaphore("S_" + n))
            SV[n] = 0

        def inc(name, instr, n=1):
            SV[name] += n
            instr.then_inc(SH[name], n)
            return SV[name]

        eyeb = e(nc.sbuf_tensor("eyeb", [128, 128], BF16))
        eyeh = e(nc.sbuf_tensor("eyeh", [128, 128], F16))
        eyef = e(nc.sbuf_tensor("eyef", [128, 128], F32))
        onesf = e(nc.sbuf_tensor("onesf", [128, 128], F32))
        ones_r = e(nc.sbuf_tensor("ones_r", [1, 128], F32))
        ones_h = e(nc.sbuf_tensor("ones_h", [128, 1], F16))
        tmp6 = e(nc.sbuf_tensor("tmp6", [128, 12], F32))
        mv1 = e(nc.sbuf_tensor("mv1", [128, 2 * NT], F32))
        m1c = e(nc.sbuf_tensor("m1c", [128, NT], F32))
        rstd1 = e(nc.sbuf_tensor("rstd1", [128, NT], F32))
        sq1 = e(nc.sbuf_tensor("sq1", [128, NT], F32))
        mv2 = e(nc.sbuf_tensor("mv2", [128, 2 * NT], F32))
        rstd2 = e(nc.sbuf_tensor("rstd2", [128, NT], F32))
        sq2 = e(nc.sbuf_tensor("sq2", [128, NT], F32))
        maxa1 = e(nc.sbuf_tensor("maxa1", [128, NT], F32))
        q1s = e(nc.sbuf_tensor("q1s", [128, NT], F32))
        s1col = e(nc.sbuf_tensor("s1col", [128, NT], F32))
        s1row = e(nc.sbuf_tensor("s1row", [1, NTS], F32))
        s1t = e(nc.sbuf_tensor("s1t", [NT, 128], F32))
        nmx = e(nc.sbuf_tensor("nmx", [128, 1], F32))
        zsum = e(nc.sbuf_tensor("zsum", [128, 1], F32))
        rZ = e(nc.sbuf_tensor("rZ", [128, NT], F32))
        swcol = e(nc.sbuf_tensor("swcol", [128, 4], F32))
        cs1 = e(nc.sbuf_tensor("cs1", [128, 32], F32))
        cs2 = e(nc.sbuf_tensor("cs2", [128, 8], F32))
        cw3 = e(nc.sbuf_tensor("cw3", [128, 96], F32))
        convb = e(nc.sbuf_tensor("convb", [128, 32], F32))
        maskL = e(nc.sbuf_tensor("maskL", [128, 1], F32))
        maskL2 = e(nc.sbuf_tensor("maskL2", [128, 1], F32))
        maskR = e(nc.sbuf_tensor("maskR", [128, 1], F32))
        maskR2 = e(nc.sbuf_tensor("maskR2", [128, 1], F32))
        bh2s = e(nc.sbuf_tensor("bh2s", [128, 1], F32))
        ba128 = e(nc.sbuf_tensor("ba128", [128, 1], F32))
        epsc = e(nc.sbuf_tensor("epsc", [128, 1], F32))
        xq1T = e(nc.sbuf_tensor("xq1T", [128, 8 * NTS], F16))
        zero5 = e(nc.sbuf_tensor("zero5", [128, 512], BF16))
        zmaskf = zero5.bitcast(F32)[:, 128:192]
        sepb = e(nc.sbuf_tensor("sepb", [128, 256], F32))

        def SEP(v):
            v.tensor_scalar(out=sepb[:], in0=sepb[:], scalar1=1.0,
                            scalar2=None, op0=ALU.mult)

        xT_stk = ExitStack()
        xT = xT_stk.enter_context(nc.sbuf_tensor("xT", [128, NT * D], F32))

        at_stk = ExitStack()
        ae = at_stk.enter_context
        xnT = ae(nc.sbuf_tensor("xnT", [128, NT * D], BF16))
        wtd = ae(nc.sbuf_tensor("wtd", [128, 8 * 128], BF16))
        wx = ae(nc.sbuf_tensor("wx", [128, 8 * U], BF16))
        wabd = ae(nc.sbuf_tensor("wabd", [128, 2], BF16))
        q2 = ae(nc.sbuf_tensor("q2", [128, NTS], BF16))
        k2 = ae(nc.sbuf_tensor("k2", [128, 64 + NTS], BF16))
        Hb = [ae(nc.sbuf_tensor(f"Hb{i}", [128, NTS], BF16)) for i in range(2)]
        e2buf = [ae(nc.sbuf_tensor(f"e2buf{i}", [2, NTS], F32)) for i in range(2)]
        xsq = ae(nc.sbuf_tensor("xsq", [128, D], F32))
        dbgpp = xq1T.bitcast(F32)[:, 5248:5248 + NT * W]
        maskat = ae(nc.sbuf_tensor("maskat", [128, 2 * W], F32))
        xnF = [ae(nc.sbuf_tensor(f"xnF{i}", [128, 128], BF16)) for i in range(2)]
        ppf = [ae(nc.sbuf_tensor(f"ppf{i}", [128, W], F32)) for i in range(2)]
        ppb = [ae(nc.sbuf_tensor(f"ppb{i}", [128, W], BF16)) for i in range(2)]
        btb = [ae(nc.sbuf_tensor(f"btb{i}", [128, 256], BF16)) for i in range(2)]
        bb = [ae(nc.sbuf_tensor(f"bb{i}", [128, 128], BF16)) for i in range(6)]
        xq1Tf32 = xq1T.bitcast(F32)
        wfb = [xq1Tf32[:, 0:1024], xq1Tf32[:, 1024:2048]]
        wqb = [xq1T[:, 4096:5120], xq1T[:, 5120:6144]]
        wt16 = xq1T[:, 6144:7168]
        ascr = [xq1T[:, 8448:9472], xq1T[:, 9472:10496]]
        asums = xq1Tf32[:, 3584:3648]
        csb = xq1Tf32[0:1, 3648:4160]
        totc = xq1Tf32[:, 4160:4162]

        block = e(nc.Block())

        class _NullBlock:
            def __getattr__(self, name):
                def deco(fn):
                    return None
                return deco

        _nullb = _NullBlock()

        def B(n):
            return block if phase >= n else _nullb

        ps0 = ExitStack()
        p0e = ps0.enter_context
        cr1 = [p0e(nc.psum_tensor(f"cr1_{i}", [128, 512], F32)) for i in range(2)]
        cr2 = [p0e(nc.psum_tensor(f"cr2_{i}", [128, 512], F32)) for i in range(2)]
        mps = p0e(nc.psum_tensor("mps", [128, 512], F32))
        tpsc = p0e(nc.psum_tensor("tpsc", [128, 512], F32))

        @B(1).sync
        def _(sync):
            for k in range(NT):
                inc("x", sync.dma_start(out=xT[:, k * D:(k + 1) * D],
                                        in_=x_in[k * 128:(k + 1) * 128, :]), 16)
            for ap_s, ap_d in [(eyeb_in, eyeb), (eyeh_in, eyeh), (eyef_in, eyef),
                               (onesf_in, onesf), (ones_r_in, ones_r),
                               (ones_h_in, ones_h), (mask_in, maskat),
                               (maskL_in, maskL), (maskL2_in, maskL2),
                               (maskR_in, maskR), (maskR2_in, maskR2),
                               (cw3_in, cw3), (cb_in, convb), (bh2_in, bh2s),
                               (ba_in, ba128), (eps_in, epsc),
                               (zero5_in, zero5)]:
                inc("c", sync.dma_start(out=ap_d[:], in_=ap_s), 16)

        @B(1).gpsimd
        def _(g):
            pass
            for k in range(8):
                inc("cg", g.dma_start(out=wtd[:, k * 128:(k + 1) * 128],
                                     in_=wtd_in[k * 128:(k + 1) * 128, :]), 16)
                inc("cg", g.dma_start(out=wx[:, k * U:(k + 1) * U],
                                     in_=wx_in[k * 128:(k + 1) * 128, :]), 16)
            inc("cg", g.dma_start(out=wabd[:], in_=wabd_in), 16)
            g.wait_ge(SH["c"], SV["c"])
            total = NT * 128 * 256
            off = 0
            while off < total:
                n = min(65536, total - off)
                rows = n // 512
                inc("zw", g.dma_start(
                    out=bass.AP(tensor=zshear, offset=off, ap=[[512, rows], [1, 512]]),
                    in_=zero5[0:rows, :]), 16)
                off += n

        w1chA = [(r, q) for r in range(8) for q in range(4)]
        w1chB = [(q, r) for q in range(4) for r in range(8)]

        @B(1).sync
        def _(sync):
            for i, (r, q) in enumerate(w1chA):
                if i >= 2:
                    sync.wait_ge(SH["wAq"], i - 1)
                inc(f"wA{i % 2}", sync.dma_start(
                    out=wfb[i % 2],
                    in_=w1T_in[r * 128:(r + 1) * 128, q * 1024:(q + 1) * 1024]), 16)
            for i in range(32):
                sync.wait_ge(SH["wAq"], 32 + i - 1)
                inc(f"wA{i % 2}", sync.dma_start(
                    out=wfb[i % 2],
                    in_=w2T_in[i * 128:(i + 1) * 128, :]), 16)

        @B(1).scalar
        def _(s):
            for i in range(64):
                s.wait_ge(SH[f"wA{i % 2}"], 16 * (i // 2 + 1))
                inc("wAq", s.activation(out=ascr[i % 2][:], in_=wfb[i % 2],
                                        func=AF.Abs, bias=0.0, scale=1.0,
                                        accum_out=asums[:, i:i + 1]))

        @B(1).vector
        def _(v):
            for c0 in range(0, 256, 64):
                v.memset(sepb[:, c0:c0 + 64], 0.0)
            v.wait_ge(SH["wAq"], 32)
            inc("sw", v.reduce_sum(totc[:, 0:1], asums[:, 0:32], axis=AX.X))
            v.wait_ge(SH["wAq"], 64)
            inc("sw", v.reduce_sum(totc[:, 1:2], asums[:, 32:64], axis=AX.X))

        @B(1).tensor
        def _(t):
            t.wait_ge(SH["c"], SV["c"])
            t.wait_ge(SH["sw"], 2)
            _ins = t.matmul(mps[:, 0:2], onesf[:], totc[:, 0:2], start=True, stop=True)
            inc("sw", _ins)

        @B(1).vector
        def _(v):
            v.wait_ge(SH["sw"], 3)
            v.tensor_scalar(out=swcol[:, 1:2], in0=mps[:, 0:1],
                            scalar1=1.0 / (D * DH), scalar2=1e-5,
                            op0=ALU.mult, op1=ALU.max)
            v.tensor_scalar(out=swcol[:, 3:4], in0=mps[:, 1:2],
                            scalar1=1.0 / (D * DH), scalar2=1e-5,
                            op0=ALU.mult, op1=ALU.max)
            SEP(v)
            v.reciprocal(swcol[:, 0:1], swcol[:, 1:2])
            v.reciprocal(swcol[:, 2:3], swcol[:, 3:4])
            _ins = v.tensor_scalar(out=sepb[:], in0=sepb[:], scalar1=1.0,
                                   scalar2=None, op0=ALU.mult)
            inc("sw", _ins)

        @B(1).sync
        def _(sync):
            for i in range(64):
                if i >= 2:
                    sync.wait_ge(SH["wBq"], i - 1)
                else:
                    sync.wait_ge(SH["wAq"], 64)
                if i < 32:
                    q, r = w1chB[i]
                    inc(f"wB{i % 2}", sync.dma_start(
                        out=wfb[i % 2],
                        in_=w1T_in[r * 128:(r + 1) * 128,
                                   q * 1024:(q + 1) * 1024]), 16)
                else:
                    ck = i - 32
                    inc(f"wB{i % 2}", sync.dma_start(
                        out=wfb[i % 2],
                        in_=w2T_in[ck * 128:(ck + 1) * 128, :]), 16)

        @B(1).vector
        def _(v):
            v.wait_ge(SH["sw"], 4)
            for i in range(64):
                scl = swcol[:, 0:1] if i < 32 else swcol[:, 2:3]
                v.wait_ge(SH[f"wB{i % 2}"], 16 * (i // 2 + 1))
                if i >= 2:
                    v.wait_ge(SH[f"wBw{i % 2}"], 16 * 8 * ((i - 2) // 2 + 1))
                    v.wait_ge(SH["csA"], i - 1)
                v.tensor_scalar(out=wt16[:], in0=wfb[i % 2][:],
                                scalar1=scl, scalar2=MAGIC,
                                op0=ALU.mult, op1=ALU.add)
                v.tensor_scalar(out=wt16[:], in0=wt16[:],
                                scalar1=MAGIC - 1.0, scalar2=MAGIC + 1.0,
                                op0=ALU.max, op1=ALU.min)
                _ins = v.tensor_scalar(out=wqb[i % 2][:], in0=wt16[:],
                                       scalar1=MAGIC, scalar2=None,
                                       op0=ALU.subtract)
                inc("wBq", _ins)
                if i < 32 and i % 8 == 7:
                    qq = i // 8
                    v.wait_ge(SH["csA"], 8 * (qq + 1))
                    if 2 * qq >= 1:
                        v.wait_ge(SH["cst"], 2 * qq)
                    _ins = v.tensor_copy(csb, cr1[0][0:1, :])
                    inc("cse", _ins)
                    v.wait_ge(SH["cst"], 2 * qq + 1)
                    _ins = v.tensor_copy(csb, cr1[1][0:1, :])
                    inc("cse", _ins)
            v.wait_ge(SH["csA"], 64)
            v.wait_ge(SH["cst"], 8)
            _ins = v.tensor_copy(csb, cr2[0][0:1, :])
            inc("cse", _ins)
            v.wait_ge(SH["cst"], 9)
            _ins = v.tensor_copy(csb, cr2[1][0:1, :])
            inc("cse", _ins)
            v.wait_ge(SH["cst"], 10)
            v.tensor_scalar(out=cs1[:], in0=tpsc[:, 0:32], scalar1=MAGIC,
                            scalar2=None, op0=ALU.mult)
            _ins = v.tensor_scalar(out=cs2[:], in0=tpsc[:, 40:48], scalar1=MAGIC,
                                   scalar2=None, op0=ALU.mult)
            inc("cst", _ins)

        @B(1).scalar
        def _(s):
            for i in range(64):
                s.wait_ge(SH["wBq"], i + 1)
                if i < 32:
                    q, r = w1chB[i]
                    for ccl in range(8):
                        cc = 8 * q + ccl
                        inc(f"wBw{i % 2}", s.dma_start(
                            out=bass.AP(tensor=w1q_d, offset=cc * 1024 + r * 128,
                                        ap=[[32768, 128], [1, 128]]),
                            in_=wqb[i % 2][:, ccl * 128:(ccl + 1) * 128]), 16)
                else:
                    ck = i - 32
                    for m in range(8):
                        inc(f"wBw{i % 2}", s.dma_start(
                            out=bass.AP(tensor=w2q_d, offset=m * 4096 + ck * 128,
                                        ap=[[32768, 128], [1, 128]]),
                            in_=wqb[i % 2][:, m * 128:(m + 1) * 128]), 16)

        @B(1).tensor
        def _(t):
            for i in range(64):
                t.wait_ge(SH["wBq"], i + 1)
                if i < 32:
                    q, r = w1chB[i]
                    if r == 0 and q >= 1:
                        t.wait_ge(SH["cse"], 2 * q)
                    for sl in range(2):
                        _ins = t.matmul(cr1[sl][0:1, :], ones_h[:, 0:1],
                                        wqb[i % 2][:, sl * 512:(sl + 1) * 512],
                                        start=(r == 0), stop=(r == 7))
                    inc("csA", _ins)
                    if r == 7:
                        for sl in range(2):
                            t.wait_ge(SH["cse"], 2 * q + sl + 1)
                            for j in range(4):
                                cc = 8 * q + 4 * sl + j
                                _ins = t.transpose(
                                    tpsc[:, cc:cc + 1],
                                    csb[0:1, j * 128:(j + 1) * 128],
                                    eyef[0:1, 0:1])
                            inc("cst", _ins)
                else:
                    ck = i - 32
                    for sl in range(2):
                        _ins = t.matmul(cr2[sl][0:1, :], ones_h[:, 0:1],
                                        wqb[i % 2][:, sl * 512:(sl + 1) * 512],
                                        start=(ck == 0), stop=(ck == 31))
                    inc("csA", _ins)
            for sl in range(2):
                t.wait_ge(SH["cse"], 8 + sl + 1)
                for j in range(4):
                    m = 4 * sl + j
                    _ins = t.transpose(tpsc[:, 40 + m:41 + m],
                                       csb[0:1, j * 128:(j + 1) * 128],
                                       eyef[0:1, 0:1])
                inc("cst", _ins)

        @B(2).scalar
        def _(s):
            s.wait_ge(SH["x"], 16 * NT)
            for i in range(NT):
                if i >= 1:
                    s.wait_ge(SH["sq1s"], i)
                _ins = s.activation(out=xsq[:], in_=xT[:, i * D:(i + 1) * D],
                                    func=AF.Square, bias=0.0, scale=1.0)
                inc("sm1", _ins)

        @B(2).vector
        def _(v):
            v.wait_ge(SH["x"], 16 * NT)
            for i in range(NT):
                v.reduce_sum(mv1[:, i:i + 1], xT[:, i * D:(i + 1) * D], axis=AX.X)
                v.wait_ge(SH["sm1"], i + 1)
                _ins = v.reduce_sum(mv1[:, NT + i:NT + i + 1], xsq[:], axis=AX.X)
                inc("sq1s", _ins)
            v.tensor_scalar(out=m1c[:], in0=mv1[:, 0:NT], scalar1=1.0 / D,
                            scalar2=None, op0=ALU.mult)
            SEP(v)
            v.tensor_mul(sq1[:], m1c[:], m1c[:])
            SEP(v)
            _ins = v.scalar_tensor_tensor(out=sq1[:], in0=mv1[:, NT:2 * NT],
                                          scalar=1.0 / D, in1=sq1[:],
                                          op0=ALU.mult, op1=ALU.subtract)
            inc("ln", _ins)

        @B(2).scalar
        def _(s):
            s.wait_ge(SH["ln"], 1)
            _ins = s.activation(out=sq1[:], in_=sq1[:], func=AF.Sqrt,
                                bias=epsc[:, 0:1], scale=1.0)
            inc("ln", _ins)

        @B(2).vector
        def _(v):
            v.wait_ge(SH["ln"], 2)
            _ins = v.reciprocal(rstd1[:], sq1[:])
            inc("ln", _ins)
            SEP(v)
            for i in range(NT):
                _ins = v.tensor_scalar(out=xnT[:, i * D:(i + 1) * D],
                                in0=xT[:, i * D:(i + 1) * D],
                                scalar1=m1c[:, i:i + 1],
                                scalar2=rstd1[:, i:i + 1],
                                op0=ALU.subtract, op1=ALU.mult)
                inc("lnr", _ins)

        ps0.close()
        psA = ExitStack()
        pAe = psA.enter_context
        tpsq = [pAe(nc.psum_tensor(f"tpsq{i}", [128, 1024], BF16)) for i in range(2)]
        qps = [pAe(nc.psum_tensor(f"qps{i}", [128, 512], F32)) for i in range(2)]
        kps = [pAe(nc.psum_tensor(f"kps{i}", [128, 512], F32)) for i in range(2)]

        @B(2).tensor
        def _(t):
            t.wait_ge(SH["cst"], 11)
            t.wait_ge(SH["cg"], SV["cg"])
            n = 0
            for i in range(NT):
                t.wait_ge(SH["lnr"], i + 1)
                for dj in range(8):
                    if n >= 2:
                        t.wait_ge(SH["qke"], n - 1)
                    _ins = t.transpose(tpsq[n % 2][:, 0:128],
                                xnT[:, i * D + dj * 128:i * D + (dj + 1) * 128],
                                eyeb[:])
                    inc("qkt", _ins)
                    t.wait_ge(SH["qke"], n + 1)
                    if dj == 0 and i >= 2:
                        t.wait_ge(SH["qev"], i - 1)
                    t.matmul(qps[i % 2][:, 0:128], wtd[:, dj * 128:(dj + 1) * 128],
                             xnF[n % 2][:], start=(dj == 0), stop=(dj == 7))
                    _ins = t.matmul(kps[i % 2][0:64, 0:128], wx[:, dj * U:(dj + 1) * U],
                             xnF[n % 2][:], start=(dj == 0), stop=(dj == 7))
                    inc("qkm", _ins)
                    n += 1

        @B(2).scalar
        def _(s):
            n = 0
            for i in range(NT):
                for dj in range(8):
                    s.wait_ge(SH["qkt"], n + 1)
                    if n >= 2:
                        s.wait_ge(SH["qkm"], n - 1)
                    _ins = s.copy(xnF[n % 2][:], tpsq[n % 2][:, 0:128])
                    inc("qke", _ins)
                    n += 1

        @B(2).vector
        def _(v):
            v.memset(k2[0:64, 0:32], 0.0)
            v.memset(k2[0:64, 32 + NTS:64 + NTS], 0.0)
            for i in range(NT):
                v.wait_ge(SH["qkm"], 8 * (i + 1))
                v.tensor_copy(q2[:, i * 128:(i + 1) * 128], qps[i % 2][:, 0:128])
                _ins = v.tensor_copy(k2[0:64, 32 + i * 128:32 + (i + 1) * 128],
                              kps[i % 2][0:64, 0:128])
                inc("qev", _ins)
            v.tensor_copy(k2[64:128, 0:63 + NTS], k2[0:64, 1:64 + NTS])
            _ins = v.memset(k2[64:128, 63 + NTS:64 + NTS], 0.0)
            inc("ksh", _ins)

        psA.close()
        psBC = ExitStack()
        pBCe = psBC.enter_context
        pp_ps = [pBCe(nc.psum_tensor(f"pp_ps{i}", [128, 512], F32)) for i in range(3)]
        psB2 = ExitStack()
        eps_t = psB2.enter_context(nc.psum_tensor("eps_t", [32, 2304], F32))

        def ppsl(i):
            return pp_ps[i // 8][:, (i % 8) * 64:(i % 8) * 64 + 64]

        def ppsl2(i, p):
            c = (i % 8) * 64 + 2 * p
            return pp_ps[i // 8][:, c:c + 2]

        @B(3).vector
        def _(v):
            for p in range(NJP):
                v.wait_ge(SH["ksh"], 1)
                if p >= 2:
                    v.wait_ge(SH["ht"], p - 1)
                    v.wait_ge(SH["he"], p - 1)
                _ins = v.tensor_add(Hb[p % 2][:], q2[:, 0:NTS], k2[:, 2 * p:2 * p + NTS])
                inc("hv", _ins)
                if p % 2 == 0:
                    v.wait_ge(SH["he"], p + 1)
                    if p >= 2:
                        v.wait_ge(SH["ptr"], p - 1)
                    _ins = v.tensor_copy(e2buf[p % 2][0:2, :], eps_t[0:2, :])
                    inc("ee", _ins)

        @B(3).scalar
        def _(s):
            for p in range(NJP):
                s.wait_ge(SH["hv"], p + 1)
                if p >= 2:
                    s.wait_ge(SH["he"], p - 1)
                _ins = s.activation(out=Hb[p % 2][:], in_=Hb[p % 2][:], func=AF.Tanh,
                             bias=bh2s[:, 0:1], scale=1.0)
                inc("ht", _ins)
                if p % 2 == 1:
                    s.wait_ge(SH["he"], p + 1)
                    if p >= 2:
                        s.wait_ge(SH["ptr"], p - 1)
                    _ins = s.copy(e2buf[p % 2][0:2, :], eps_t[0:2, :])
                    inc("ee", _ins)

        @B(3).tensor
        def _(t):
            t.wait_ge(SH["qev"], NT)
            for p in range(NJP):
                t.wait_ge(SH["ht"], p + 1)
                t.wait_ge(SH["ee"], p)
                for c in range(5):
                    _ins = t.matmul(eps_t[0:2, EO[c]:EO[c] + EW[c]], wabd[:],
                             Hb[p % 2][:, EO[c]:EO[c] + EW[c]],
                             start=True, stop=True)
                inc("he", _ins)
                t.wait_ge(SH["ee"], p + 1)
                for i in range(NT):
                    _ins = t.transpose(ppsl2(i, p),
                                e2buf[p % 2][0:2, i * 128:(i + 1) * 128],
                                eyef[0:2, 0:2])
                inc("ptr", _ins)

        psB2.close()
        psB3 = ExitStack()
        pB3e = psB3.enter_context
        btps = [pB3e(nc.psum_tensor(f"btps{i}", [128, 1024], BF16)) for i in range(2)]
        vps = [pB3e(nc.psum_tensor(f"vps{n}", [128, 512], F32)) for n in range(2)]

        @B(4).tensor
        def _(t):
            for i in range(NT):
                if sub < 2:
                    continue
                t.wait_ge(SH[f"shr{i % 2}"], 16 * (i // 2 + 1))
                if i >= 2:
                    t.wait_ge(SH["bev"], i - 1)
                    t.wait_ge(SH["bev2"], i - 1)
                t.transpose(btps[i % 2][0:64, 0:128], btb[i % 2][:, 0:64], eyeb[:])
                t.transpose(btps[i % 2][0:128, 128:256], btb[i % 2][:, 64:192],
                            eyeb[:])
                _ins = t.transpose(btps[i % 2][0:64, 256:384], btb[i % 2][:, 192:256],
                            eyeb[:])
                inc("btr", _ins)
                if sub < 3:
                    continue
                t.wait_ge(SH["bev"], i + 1)
                t.wait_ge(SH["bev2"], i + 1)
                if i >= 1:
                    t.wait_ge(SH["vev"], i)
                for n in range(2):
                    pieces = []
                    if i >= 1:
                        pieces.append((bb[(3 * i) % 6][64:128, :],
                                       xnT[64:128, (i - 1) * D + n * 512:
                                           (i - 1) * D + n * 512 + 512]))
                    pieces.append((bb[(3 * i + 1) % 6][0:128, :],
                                   xnT[:, i * D + n * 512:i * D + n * 512 + 512]))
                    if i <= NT - 2:
                        pieces.append((bb[(3 * i + 2) % 6][0:64, :],
                                       xnT[0:64, (i + 1) * D + n * 512:
                                           (i + 1) * D + n * 512 + 512]))
                    for pi, (lhsT, rhs) in enumerate(pieces):
                        _ins = t.matmul(vps[n][:, 0:512], lhsT, rhs,
                                 start=(pi == 0), stop=(pi == len(pieces) - 1))
                inc("vmm", _ins)

        @B(4).vector
        def _(v):
            v.wait_ge(SH["ptr"], NJP)
            for i in range(NT):
                if serial and i >= 1:
                    v.wait_ge(SH["vev"], i)
                msl = (maskat[:, 0:W] if i == 1 else
                       (maskat[:, W:2 * W] if i == 16 else zmaskf[:, 0:W]))
                v.scalar_tensor_tensor(out=ppf[i % 2][:], in0=ppsl(i),
                                       scalar=ba128[:, 0:1], in1=msl,
                                       op0=ALU.add, op1=ALU.add)
                SEP(v)
                _ins = v.reduce_max(nmx[:, 0:1], ppf[i % 2][:], axis=AX.X, negate=True)
                inc("mq", _ins)
                v.wait_ge(SH["pex"], i + 1)
                if i >= 2 and sub >= 1:
                    v.wait_ge(SH[f"shw{i % 2}"], 16 * (i // 2))
                v.reduce_sum(zsum[:, 0:1], ppf[i % 2][:], axis=AX.X)
                _ins = v.tensor_copy(ppb[i % 2][:], ppf[i % 2][:])
                inc("pcv", _ins)
                SEP(v)
                v.reciprocal(rZ[:, i:i + 1], zsum[:, 0:1])
                if dbg:
                    v.tensor_copy(dbgpp[:, i * W:(i + 1) * W], ppf[i % 2][:])
                if sub >= 2:
                    v.wait_ge(SH["bev"], i + 1)
                    _ins = v.tensor_copy(bb[(3 * i) % 6][64:128, :],
                                         btps[i % 2][0:64, 0:128])
                    inc("bev2", _ins)
                else:
                    SEP(v)
                if sub >= 4:
                    v.wait_ge(SH["vmm"], i + 1)
                    for n in range(2):
                        _ins = v.scalar_tensor_tensor(
                            out=xT[:, i * D + n * 512:i * D + n * 512 + 512],
                            in0=vps[n][:, 0:512],
                            scalar=rZ[:, i:i + 1],
                            in1=xT[:, i * D + n * 512:i * D + n * 512 + 512],
                            op0=ALU.mult, op1=ALU.add)
                    inc("vev", _ins)
                else:
                    inc("vev", v.nop())

        @B(4).scalar
        def _(s):
            for i in range(NT):
                s.wait_ge(SH["mq"], i + 1)
                _ins = s.activation(out=ppf[i % 2][:], in_=ppf[i % 2][:], func=AF.Exp,
                             bias=nmx[:, 0:1], scale=1.0)
                inc("pex", _ins)
                if sub >= 2:
                    s.wait_ge(SH["btr"], i + 1)
                    if i >= 2 and sub >= 3:
                        s.wait_ge(SH["vmm"], i - 1)
                    s.copy(bb[(3 * i + 1) % 6][0:128, :],
                           btps[i % 2][0:128, 128:256])
                    _ins = s.copy(bb[(3 * i + 2) % 6][0:64, :],
                                  btps[i % 2][0:64, 256:384])
                    inc("bev", _ins)

        @B(4).gpsimd
        def _(g):
            if sub < 1:
                g.nop()
                return
            g.wait_ge(SH["zw"], SV["zw"])
            for i in range(NT):
                g.wait_ge(SH["pcv"], i + 1)
                inc(f"shw{i % 2}", g.dma_start(
                    out=bass.AP(tensor=zshear, offset=i * 32768 + 32,
                                ap=[[257, 128], [1, 64]]),
                    in_=ppb[i % 2][:]), 16)

        @B(4).sync
        def _(sync):
            if sub < 1:
                sync.nop()
                return
            sync.wait_ge(SH["zw"], SV["zw"])
            for i in range(NT):
                sync.wait_ge(SH[f"shw{i % 2}"], 16 * (i // 2 + 1))
                if i >= 2 and sub >= 2:
                    sync.wait_ge(SH["btr"], i - 1)
                inc(f"shr{i % 2}", sync.dma_start(
                    out=btb[i % 2][:],
                    in_=bass.AP(tensor=zshear, offset=i * 32768,
                                ap=[[256, 128], [1, 256]])), 16)

        if dbg:
            @block.gpsimd
            def _(g):
                g.wait_ge(SH["shw0"], SV["shw0"])
                g.wait_ge(SH["shw1"], SV["shw1"])
                inc("dbgs", g.dma_start(
                    out=dbg_t["d_zt"],
                    in_=bass.AP(tensor=zshear, offset=0,
                                ap=[[256, 18 * 128], [1, 256]])), 16)

            @block.sync
            def _(sync):
                sync.wait_ge(SH["lnr"], NT)
                inc("dbgs", sync.dma_start(out=dbg_t["d_xn"], in_=xnT[:, 4 * D:5 * D]), 16)
                sync.wait_ge(SH["ksh"], 1)
                inc("dbgs", sync.dma_start(out=dbg_t["d_q2"], in_=q2[:]), 16)
                inc("dbgs", sync.dma_start(out=dbg_t["d_k2"], in_=k2[:]), 16)
                sync.wait_ge(SH["pcv"], 6)
                inc("dbgs", sync.dma_start(out=dbg_t["d_pp"], in_=ppf[1][:]), 16)
                sync.wait_ge(SH["shr1"], 16 * 3)
                inc("dbgs", sync.dma_start(out=dbg_t["d_bt"],
                                           in_=btb[1][:]), 16)
                sync.wait_ge(SH["vev"], NT)
                inc("dbgs", sync.dma_start(out=dbg_t["d_ppall"], in_=dbgpp[:]), 16)
                inc("dbgs", sync.dma_start(out=dbg_t["d_x2"], in_=xT[:, 4 * D:5 * D]), 16)
                inc("dbgs", sync.dma_start(out=dbg_t["d_rz"], in_=rZ[:]), 16)
                inc("dbgs", sync.dma_start(out=dbg_t["d_sw"], in_=swcol[:]), 16)
                sync.wait_ge(SH["cst"], 11)
                inc("dbgs", sync.dma_start(out=dbg_t["d_cs1"], in_=cs1[:]), 16)

        psB3.close()
        psBC.close()
        p3p_stk = ExitStack()
        p3pe = p3p_stk.enter_context
        tqp = [p3pe(nc.psum_tensor(f"tqp{i}", [128, 1024], F16)) for i in range(2)]
        s1ps = p3pe(nc.psum_tensor("s1ps", [128, 512], F32))

        at_stk.close()
        p3s_stk = ExitStack()
        p3se = p3s_stk.enter_context
        xn2 = [p3se(nc.sbuf_tensor(f"xn2_{i}", [128, D], F32)) for i in range(4)]
        xq1t = [p3se(nc.sbuf_tensor(f"xq1t_{i}", [128, D], F16)) for i in range(2)]

        @B(5).scalar
        def _(s):
            s.wait_ge(SH["vev"], NT)
            for i in range(NT):
                if i >= 1:
                    s.wait_ge(SH["sq2s"], i)
                _ins = s.activation(out=xsq[:], in_=xT[:, i * D:(i + 1) * D],
                                    func=AF.Square, bias=0.0, scale=1.0)
                inc("sm2", _ins)

        @B(5).vector
        def _(v):
            v.wait_ge(SH["vev"], NT)
            v.wait_ge(SH["shw0"], SV["shw0"])
            v.wait_ge(SH["shw1"], SV["shw1"])
            for i in range(NT):
                v.reduce_sum(mv2[:, i:i + 1], xT[:, i * D:(i + 1) * D], axis=AX.X)
                v.wait_ge(SH["sm2"], i + 1)
                _ins = v.reduce_sum(mv2[:, NT + i:NT + i + 1], xsq[:], axis=AX.X)
                inc("sq2s", _ins)
            v.tensor_scalar(out=mv2[:, 0:NT],
                            in0=mv2[:, 0:NT], scalar1=1.0 / D,
                            scalar2=None, op0=ALU.mult)
            SEP(v)
            v.tensor_mul(sq2[:], mv2[:, 0:NT], mv2[:, 0:NT])
            SEP(v)
            _ins = v.scalar_tensor_tensor(out=sq2[:], in0=mv2[:, NT:2 * NT],
                                          scalar=1.0 / D, in1=sq2[:],
                                          op0=ALU.mult, op1=ALU.subtract)
            inc("st2", _ins)

        @B(5).scalar
        def _(s):
            s.wait_ge(SH["st2"], 1)
            _ins = s.activation(out=sq2[:], in_=sq2[:], func=AF.Sqrt,
                                bias=epsc[:, 0:1], scale=1.0)
            inc("st2", _ins)

        @B(5).vector
        def _(v):
            v.wait_ge(SH["st2"], 2)
            _ins = v.reciprocal(rstd2[:], sq2[:])
            inc("st2", _ins)
            SEP(v)
            for i in range(NT + 3):
                if i < NT:
                    v.tensor_scalar(out=xn2[i % 4][:], in0=xT[:, i * D:(i + 1) * D],
                                    scalar1=mv2[:, i:i + 1],
                                    scalar2=rstd2[:, i:i + 1],
                                    op0=ALU.subtract, op1=ALU.mult)
                    v.reduce_max(maxa1[:, i:i + 1], xn2[i % 4][:], axis=AX.X,
                                 apply_absolute_value=True)
                if 1 <= i <= NT:
                    j = i - 1
                    v.tensor_scalar(out=maxa1[:, j:j + 1], in0=maxa1[:, j:j + 1],
                                    scalar1=1e-5, scalar2=1.0 / 127.0,
                                    op0=ALU.max, op1=ALU.mult)
                if 2 <= i <= NT + 1:
                    j = i - 2
                    v.reciprocal(q1s[:, j:j + 1], maxa1[:, j:j + 1])
                if i >= 3:
                    j = i - 3
                    if j >= 2:
                        v.wait_ge(SH["tqt"], j - 1)
                    _ins = v.tensor_scalar(out=xq1t[j % 2][:], in0=xn2[j % 4][:],
                                           scalar1=q1s[:, j:j + 1], scalar2=MAGIC,
                                           op0=ALU.mult, op1=ALU.add)
                    if j == 0:
                        _ins = v.tensor_scalar(out=xq1t[j % 2][:], in0=xq1t[j % 2][:],
                                               scalar1=maskL[:, 0:1],
                                               scalar2=maskL2[:, 0:1],
                                               op0=ALU.mult, op1=ALU.add)
                    if j == NT - 1:
                        _ins = v.tensor_scalar(out=xq1t[j % 2][:], in0=xq1t[j % 2][:],
                                               scalar1=maskR[:, 0:1],
                                               scalar2=maskR2[:, 0:1],
                                               op0=ALU.mult, op1=ALU.add)
                    inc("tq1", _ins)
            _ins = v.tensor_scalar(out=s1col[:], in0=maxa1[:], scalar1=swcol[:, 1:2],
                            scalar2=None, op0=ALU.mult)
            inc("s1p", _ins)

        @B(5).tensor
        def _(t):
            for i in range(NT):
                t.wait_ge(SH["tq1"], i + 1)
                if i >= 2:
                    t.wait_ge(SH["tqe"], 8 * (i - 1))
                for dj in range(8):
                    _ins = t.transpose(tqp[i % 2][:, dj * 128:(dj + 1) * 128],
                                xq1t[i % 2][:, dj * 128:(dj + 1) * 128], eyeh[:])
                inc("tqt", _ins)
            t.wait_ge(SH["s1p"], 1)
            _ins = t.transpose(s1ps[0:NT, 0:128], s1col[:], eyef[:])
            inc("s1p", _ins)

        @B(5).scalar
        def _(s):
            s.wait_ge(SH["wBw0"], SV["wBw0"])
            s.wait_ge(SH["wBw1"], SV["wBw1"])
            s.wait_ge(SH["csA"], 64)
            for i in range(NT):
                s.wait_ge(SH["tqt"], i + 1)
                for dj in range(8):
                    _ins = s.copy(xq1T[:, dj * NTS + i * 128:dj * NTS + (i + 1) * 128],
                           tqp[i % 2][:, dj * 128:(dj + 1) * 128])
                    inc("tqe", _ins)

        @B(5).vector
        def _(v):
            v.wait_ge(SH["s1p"], 2)
            _ins = v.tensor_copy(s1t[:], s1ps[0:NT, 0:128])
            inc("s1e", _ins)

        @B(5).gpsimd
        def _(g):
            g.wait_ge(SH["s1e"], 1)
            inc("s1d", g.dma_start(out=s1row[:], in_=s1t[:]), 16)

        @B(5).sync
        def _(sync):
            for i in range(1, 17):
                sync.wait_ge(SH["tq1"], i + 1)
                inc("spl", sync.dma_start(
                    out=x2_d[(i - 1) * 128:i * 128, :],
                    in_=xT[:, i * D:(i + 1) * D]), 16)

        if dbg:
            @block.sync
            def _(sync):
                sync.wait_ge(SH["tq1"], 5)
                inc("dbgs", sync.dma_start(out=dbg_t["d_xq1"], in_=xq1t[0][:]), 16)
                sync.wait_ge(SH["s1d"], 16)
                inc("dbgs", sync.dma_start(out=dbg_t["d_s1"], in_=s1row[:]), 16)

        @B(6).tensor
        def _(t):
            t.wait_ge(SH["tqe"], 8 * NT)
            inc("bar", t.nop())

        @B(6).vector
        def _(v):
            v.wait_ge(SH["s1d"], 16)
            v.wait_ge(SH["spl"], 16 * 16)
            v.wait_ge(SH["tqe"], 8 * NT)
            v.wait_ge(SH["wBw0"], SV["wBw0"])
            v.wait_ge(SH["wBw1"], SV["wBw1"])
            v.wait_ge(SH["vev"], NT)
            inc("bar", v.nop())

        @B(6).scalar
        def _(s):
            s.wait_ge(SH["bar"], 2)
            inc("bar", s.nop())

        @B(6).gpsimd
        def _(g):
            g.wait_ge(SH["bar"], 3)
            inc("bar", g.nop())

        @B(6).sync
        def _(sync):
            sync.wait_ge(SH["bar"], 4)
            inc("bar", sync.nop())

        BAR = SV["bar"]

        if phase < 6:
            @block.sync
            def _(sync):
                if phase >= 1:
                    sync.wait_ge(SH["wBw0"], SV["wBw0"])
                    sync.wait_ge(SH["wBw1"], SV["wBw1"])
                    sync.wait_ge(SH["cst"], 11)
                    sync.wait_ge(SH["zw"], SV["zw"])
                if phase >= 2:
                    sync.wait_ge(SH["lnr"], NT)
                    sync.wait_ge(SH["ksh"], 1)
                if phase >= 3:
                    sync.wait_ge(SH["ptr"], NJP)
                if phase >= 4:
                    sync.wait_ge(SH["vev"], NT)
                    sync.wait_ge(SH["shr0"], SV["shr0"])
                    sync.wait_ge(SH["shr1"], SV["shr1"])
                if phase >= 5:
                    sync.wait_ge(SH["tqe"], 8 * NT)
                    sync.wait_ge(SH["spl"], SV["spl"])
                    sync.wait_ge(SH["s1d"], 16)
                stm = SH["yo0"]
                n = 0
                if phase != 2:
                    for i in range(1, 17):
                        sync.dma_start(out=y_out[(i - 1) * 128:i * 128, :],
                                       in_=xT[:, i * D:(i + 1) * D]).then_inc(stm, 16)
                        n += 16
                    sync.wait_ge(stm, n)
            if phase == 2:
                @block.gpsimd
                def _(g):
                    g.wait_ge(SH["lnr"], NT)
                    g.wait_ge(SH["ksh"], 1)
                    n = 0
                    for rep in range(2):
                        for i in range(1, 8):
                            g.dma_start(
                                out=y_out[rep * 896 + (i - 1) * 128:
                                          rep * 896 + i * 128, :],
                                in_=xnT[:, i * D:(i + 1) * D]).then_inc(SH["yo1"], 16)
                            n += 16
                    for c0, w_, src in [(0, 2 * NT, mv1[:, 0:2 * NT]),
                                        (64, NT, m1c[:, 0:NT]),
                                        (96, NT, sq1[:, 0:NT]),
                                        (128, NT, rstd1[:, 0:NT])]:
                        g.dma_start(out=y_out[1792:1920, c0:c0 + w_],
                                    in_=src).then_inc(SH["yo1"], 16)
                        n += 16
                    g.wait_ge(SH["yo1"], n)
            p3s_stk.close()
            p3p_stk.close()
            xT_stk.close()
            return nc

        p3p_stk.close()
        p3s_stk.close()
        xT_stk.close()

        p4_stk = ExitStack()
        p4e = p4_stk.enter_context
        w2qr = p4e(nc.sbuf_tensor("w2qr", [128, 8 * 32 * 128], F16))
        w1b = [p4e(nc.sbuf_tensor(f"w1b{i}", [128, 1024], F16)) for i in range(2)]
        hbuf = p4e(nc.sbuf_tensor("hbuf", [128, 32 * CW], F32))
        convt = [p4e(nc.sbuf_tensor(f"convt{i}", [128, TCH], F32)) for i in range(2)]
        xq2 = p4e(nc.sbuf_tensor("xq2", [128, 32 * TCH], F16))
        x2r = [p4e(nc.sbuf_tensor(f"x2r{i}", [128, D], F32)) for i in range(2)]
        outT = [p4e(nc.sbuf_tensor(f"outT{i}", [128, D], F32)) for i in range(2)]
        s1bc = p4e(nc.sbuf_tensor("s1bc", [128, CW], F32))
        s2bc = p4e(nc.sbuf_tensor("s2bc", [128, TCH], F32))
        runmax = p4e(nc.sbuf_tensor("runmax", [128, TCH], F32))
        mx2 = p4e(nc.sbuf_tensor("mx2", [128, 2], F32))
        q2sb = p4e(nc.sbuf_tensor("q2sb", [128, 2], F32))
        so2 = p4e(nc.sbuf_tensor("so2", [128, 2], F32))
        s2ra = p4e(nc.sbuf_tensor("s2ra", [1, 128], F32))
        s2rb = p4e(nc.sbuf_tensor("s2rb", [1, 128], F32))
        o2sb = p4e(nc.sbuf_tensor("o2sb", [128, 8 * TCH], F32))

        ps4 = ExitStack()
        p4pe = ps4.enter_context
        m1ps = [p4pe(nc.psum_tensor(f"m1ps{i}", [128, 512], F32)) for i in range(2)]
        m2ps = p4pe(nc.psum_tensor("m2ps", [128, 2048], F32))
        otmisc = p4pe(nc.psum_tensor("otmisc", [128, 512], F32))
        ottr = p4pe(nc.psum_tensor("ottr", [128, 512], F32))

        @B(6).sync
        def _(sync):
            sync.wait_ge(SH["bar"], BAR)
            inc("w2l", sync.dma_start(out=w2qr[:], in_=w2q_d[:, :]), 16)
            for tc in range(NCH):
                for cc in range(32):
                    n = tc * 32 + cc
                    if n >= 2:
                        sync.wait_ge(SH["m1"], n - 1)
                    inc(f"w1l{cc % 2}", sync.dma_start(
                        out=w1b[cc % 2][:],
                        in_=w1q_d[:, cc * 1024:(cc + 1) * 1024]), 16)

        @B(6).gpsimd
        def _(g):
            g.wait_ge(SH["bar"], BAR)
            for tc in range(NCH):
                for h in range(2):
                    if tc >= 1:
                        g.wait_ge(SH["e2g"], 4 * (tc - 1) + 2 * (h + 1))
                    inc(f"xrl{h}", g.dma_start(
                        out=x2r[h][:],
                        in_=x2_d[tc * TCH + h * 128:tc * TCH + h * 128 + 128, :]), 16)

        @B(6).tensor
        def _(t):
            t.wait_ge(SH["bar"], BAR)
            for tc in range(NCH):
                if tc >= 1:
                    t.wait_ge(SH["s2v"], tc)
                _ins = t.matmul(otmisc[:, 0:CW], ones_r[:],
                         s1row[0:1, TCH * tc + 127:TCH * tc + 127 + CW],
                         start=True, stop=True)
                inc("s1p", _ins)
                for cc in range(32):
                    n = tc * 32 + cc
                    t.wait_ge(SH[f"w1l{cc % 2}"], 16 * (n // 2 + 1))
                    if n >= 2:
                        tp, cp = divmod(n - 2, 32)
                        t.wait_ge(SH["h1"], 33 * tp + cp + 2)
                    for r in range(8):
                        _ins = t.matmul(m1ps[cc % 2][:, 0:CW],
                                 w1b[cc % 2][:, r * 128:(r + 1) * 128],
                                 xq1T[:, r * NTS + TCH * tc + 127:
                                      r * NTS + TCH * tc + 127 + CW],
                                 start=(r == 0), stop=(r == 7))
                    inc("m1", _ins)
                t.wait_ge(SH["mxv"], tc + 1)
                t.transpose(otmisc[:, 0:128], runmax[:, 0:128], eyef[:])
                _ins = t.transpose(otmisc[:, 128:256], runmax[:, 128:256], eyef[:])
                inc("mxt", _ins)
                t.wait_ge(SH["mxq"], tc + 1)
                t.transpose(otmisc[0:1, 256:384], q2sb[:, 0:1], eyef[:])
                _ins = t.transpose(otmisc[0:1, 384:512], q2sb[:, 1:2], eyef[:])
                inc("mxt2", _ins)
                t.wait_ge(SH["s2e"], tc + 1)
                t.matmul(otmisc[:, 0:128], ones_r[:], s2ra[0:1, :],
                         start=True, stop=True)
                _ins = t.matmul(otmisc[:, 128:256], ones_r[:], s2rb[0:1, :],
                         start=True, stop=True)
                inc("s2p", _ins)
                if tc >= 1:
                    t.wait_ge(SH["e1"], 8 * tc)
                t.wait_ge(SH["w2l"], 16)
                for m in range(8):
                    for ck in range(32):
                        t.wait_ge(SH["qml"], tc * 32 + (ck + 1 if m == 0 else 32))
                        _ins = t.matmul(m2ps[:, m * TCH:(m + 1) * TCH],
                                 w2qr[:, (m * 32 + ck) * 128:(m * 32 + ck + 1) * 128],
                                 xq2[:, ck * TCH:(ck + 1) * TCH],
                                 start=(ck == 0), stop=(ck == 31))
                inc("m2", _ins)
                for gi in range(4):
                    h, mbase = gi // 2, (gi % 2) * 4
                    for mm in range(4):
                        m = mbase + mm
                        t.wait_ge(SH["e1"], tc * 8 + m + 1)
                        if tc * 4 + gi >= 1:
                            t.wait_ge(SH["e2g"], tc * 4 + gi)
                        _ins = t.transpose(ottr[:, mm * 128:(mm + 1) * 128],
                                    o2sb[:, m * TCH + h * 128:m * TCH + h * 128 + 128],
                                    eyef[:])
                    inc("otr", _ins)

        @B(6).scalar
        def _(s):
            s.wait_ge(SH["bar"], BAR)
            for tc in range(NCH):
                if tc >= 1:
                    for h in range(2):
                        s.wait_ge(SH["e2g"], 4 * (tc - 1) + 2 * (h + 1))
                        inc(f"yo{h}", s.dma_start(
                            out=y_out[(tc - 1) * TCH + h * 128:
                                      (tc - 1) * TCH + h * 128 + 128, :],
                            in_=outT[h][:]), 16)
                for cc in range(32):
                    n = tc * 32 + cc
                    s.wait_ge(SH["h1"], 33 * tc + cc + 2)
                    if n >= 2:
                        s.wait_ge(SH["cvb"], n - 1)
                    _ins = s.activation(out=convt[cc % 2][:],
                                 in_=hbuf[:, cc * CW + 1:cc * CW + 1 + TCH],
                                 func=AF.Identity, bias=convb[:, cc:cc + 1],
                                 scale=cw3[:, 32 + cc:33 + cc])
                    inc("cva", _ins)
                    s.wait_ge(SH["cvb"], n + 1)
                    _ins = s.activation(out=hbuf[:, cc * CW + 1:cc * CW + 1 + TCH],
                                 in_=convt[cc % 2][:], func=AF.Silu,
                                 bias=0.0, scale=1.0)
                    inc("sil", _ins)
            for h in range(2):
                s.wait_ge(SH["e2g"], 4 * (NCH - 1) + 2 * (h + 1))
                inc(f"yo{h}", s.dma_start(
                    out=y_out[(NCH - 1) * TCH + h * 128:
                              (NCH - 1) * TCH + h * 128 + 128, :],
                    in_=outT[h][:]), 16)
            s.wait_ge(SH["yo0"], 16 * NCH)
            s.wait_ge(SH["yo1"], 16 * NCH)

        @B(6).vector
        def _(v):
            v.wait_ge(SH["bar"], BAR)
            for tc in range(NCH):
                v.wait_ge(SH["s1p"], 2 + tc + 1)
                _ins = v.tensor_copy(s1bc[:], otmisc[:, 0:CW])
                inc("h1", _ins)
                for cc in range(32):
                    n = tc * 32 + cc
                    v.wait_ge(SH["m1"], n + 1)
                    _ins = v.scalar_tensor_tensor(out=hbuf[:, cc * CW:cc * CW + CW],
                                           in0=m1ps[cc % 2][:, 0:CW],
                                           scalar=cs1[:, cc:cc + 1],
                                           in1=s1bc[:], op0=ALU.subtract,
                                           op1=ALU.mult)
                    inc("h1", _ins)
                    v.wait_ge(SH["cva"], n + 1)
                    v.scalar_tensor_tensor(out=convt[cc % 2][:],
                                           in0=hbuf[:, cc * CW:cc * CW + TCH],
                                           scalar=cw3[:, cc:cc + 1],
                                           in1=convt[cc % 2][:],
                                           op0=ALU.mult, op1=ALU.add)
                    _ins = v.scalar_tensor_tensor(out=convt[cc % 2][:],
                                           in0=hbuf[:, cc * CW + 2:cc * CW + 2 + TCH],
                                           scalar=cw3[:, 64 + cc:65 + cc],
                                           in1=convt[cc % 2][:],
                                           op0=ALU.mult, op1=ALU.add)
                    inc("cvb", _ins)
                    v.wait_ge(SH["sil"], n + 1)
                    if cc == 0:
                        v.tensor_copy(runmax[:], hbuf[:, 1:1 + TCH])
                    else:
                        _ins = v.tensor_tensor(out=runmax[:], in0=runmax[:],
                                        in1=hbuf[:, cc * CW + 1:cc * CW + 1 + TCH],
                                        op=ALU.max)
                        if cc == 31:
                            inc("mxv", _ins)
                v.wait_ge(SH["mxt"], tc + 1)
                v.reduce_max(mx2[:, 0:1], otmisc[:, 0:128], axis=AX.X)
                v.reduce_max(mx2[:, 1:2], otmisc[:, 128:256], axis=AX.X)
                SEP(v)
                v.tensor_scalar(out=mx2[:], in0=mx2[:], scalar1=1e-5,
                                scalar2=None, op0=ALU.max)
                SEP(v)
                v.reciprocal(q2sb[:], mx2[:])
                v.tensor_scalar(out=so2[:], in0=mx2[:], scalar1=swcol[:, 3:4],
                                scalar2=1.0 / 127.0, op0=ALU.mult, op1=ALU.mult)
                SEP(v)
                _ins = v.tensor_scalar(out=q2sb[:], in0=q2sb[:], scalar1=127.0,
                                scalar2=None, op0=ALU.mult)
                inc("mxq", _ins)
                v.wait_ge(SH["mxt2"], tc + 1)
                v.tensor_copy(s2ra[0:1, :], otmisc[0:1, 256:384])
                _ins = v.tensor_copy(s2rb[0:1, :], otmisc[0:1, 384:512])
                inc("s2e", _ins)
                v.wait_ge(SH["s2p"], tc + 1)
                _ins = v.tensor_copy(s2bc[:], otmisc[:, 0:TCH])
                inc("s2v", _ins)
                for cc in range(32):
                    v.tensor_mul(convt[cc % 2][:],
                                 hbuf[:, cc * CW + 1:cc * CW + 1 + TCH], s2bc[:])
                    _ins = v.tensor_scalar(out=xq2[:, cc * TCH:(cc + 1) * TCH],
                                    in0=convt[cc % 2][:], scalar1=MAGIC,
                                    scalar2=None, op0=ALU.add)
                    inc("qml", _ins)
                v.wait_ge(SH["m2"], tc + 1)
                for m in range(8):
                    _ins = v.tensor_scalar(out=o2sb[:, m * TCH:(m + 1) * TCH],
                                    in0=m2ps[:, m * TCH:(m + 1) * TCH],
                                    scalar1=cs2[:, m:m + 1], scalar2=None,
                                    op0=ALU.subtract)
                    inc("e1", _ins)
                for gi in range(4):
                    h, mbase = gi // 2, (gi % 2) * 4
                    v.wait_ge(SH["otr"], tc * 4 + gi + 1)
                    v.wait_ge(SH[f"xrl{h}"], 16 * (tc + 1))
                    if tc >= 1 and gi % 2 == 0:
                        v.wait_ge(SH[f"yo{h}"], 16 * tc)
                    for mm in range(4):
                        m = mbase + mm
                        _ins = v.scalar_tensor_tensor(
                            out=outT[h][:, m * 128:(m + 1) * 128],
                            in0=ottr[:, mm * 128:(mm + 1) * 128],
                            scalar=so2[:, h:h + 1],
                            in1=x2r[h][:, m * 128:(m + 1) * 128],
                            op0=ALU.mult, op1=ALU.add)
                    inc("e2g", _ins)



        if dbg:
            @block.sync
            def _(sync):
                sync.wait_ge(SH["h1"], 3)
                inc("dbgs", sync.dma_start(out=dbg_t["d_h"], in_=hbuf[:, 0:CW]), 16)
                sync.wait_ge(SH["sil"], 1)
                inc("dbgs", sync.dma_start(out=dbg_t["d_cv"],
                                           in_=hbuf[:, 1:1 + TCH]), 16)
                sync.wait_ge(SH["qml"], 1)
                inc("dbgs", sync.dma_start(out=dbg_t["d_xq2"],
                                           in_=xq2[:, 0:TCH]), 16)
                sync.wait_ge(SH["s2v"], 1)
                inc("dbgs", sync.dma_start(out=dbg_t["d_s2bc"], in_=s2bc[:]), 16)

        ps4.close()
        p4_stk.close()
    return nc


def _mk_consts():
    import ml_dtypes as md
    consts = {
        "i_eyeb": np.eye(128, dtype=np.float32).astype(md.bfloat16),
        "i_eyeh": np.eye(128, dtype=np.float32).astype(np.float16),
        "i_eyef": np.eye(128, dtype=np.float32),
        "i_onesf": np.ones((128, 128), dtype=np.float32),
        "i_ones_r": np.ones((1, 128), dtype=np.float32),
        "i_ones_h": np.ones((128, 1), dtype=np.float32).astype(np.float16),
    }
    return consts


def _core_inputs(core, x_flat, Wt, Wx, bh, Wa, ba, w1, conv_w, conv_b, w2):
    g0 = core * TOK
    seq = core // 2
    s_lo, s_hi = seq * T_SEQ, (seq + 1) * T_SEQ
    lo, hi = g0 - 128, g0 + NTS - 128
    xp = np.zeros((NTS, D), dtype=np.float32)
    a, b = max(lo, s_lo), min(hi, s_hi)
    xp[a - lo:b - lo] = x_flat[a:b]

    wtd = np.concatenate([Wt, Wt], axis=1).astype(np.float32)
    wabd = np.zeros((128, 2), dtype=np.float32)
    wabd[0:64, 0] = Wa[:, 0]
    wabd[64:128, 1] = Wa[:, 0]
    bh2 = np.concatenate([bh, bh]).reshape(128, 1).astype(np.float32)
    ba128 = np.full((128, 1), float(ba[0]), dtype=np.float32)

    mask = np.zeros((128, 2 * W), dtype=np.float32)
    left_edge = (core % 2 == 0)
    right_edge = (core % 2 == 1)
    if left_edge:
        for tau in range(32):
            mask[tau, 0:(32 - tau)] = -10000.0
    if right_edge:
        for tau in range(97, 128):
            mask[tau, W + (160 - tau):2 * W] = -10000.0
    mL = np.ones((128, 1), dtype=np.float32)
    mR = np.ones((128, 1), dtype=np.float32)
    if left_edge:
        mL[127, 0] = 0.0
    if right_edge:
        mR[0, 0] = 0.0
    mL2 = MAGIC * (1.0 - mL)
    mR2 = MAGIC * (1.0 - mR)

    cw3 = np.zeros((128, 96), dtype=np.float32)
    cb = np.zeros((128, 32), dtype=np.float32)
    for cck in range(32):
        for k in range(3):
            cw3[:, 32 * k + cck] = conv_w[cck * 128:(cck + 1) * 128, 0, k]
        cb[:, cck] = conv_b[cck * 128:(cck + 1) * 128]

    d = {
        "x": xp, "i_wtd": wtd, "i_wx": np.ascontiguousarray(Wx, dtype=np.float32),
        "i_wabd": wabd, "i_bh2": bh2, "i_ba128": ba128,
        "i_maskattn": mask, "i_maskL": mL, "i_maskL2": mL2,
        "i_maskR": mR, "i_maskR2": mR2,
        "i_cw3": cw3, "i_convb": cb,
        "w1T": np.ascontiguousarray(w1.T, dtype=np.float32),
        "w2T": np.ascontiguousarray(w2.T, dtype=np.float32),
        "i_eps": np.full((128, 1), 1e-5, dtype=np.float32),
        "i_zero5": np.zeros((128, 512), dtype=np.float32).astype(ml_dtypes.bfloat16),
    }
    d.update(_mk_consts())
    return d


T_SEQ = 4096
_NC_CACHE = {}


def kernel(L0_attn_x, L0_attn_Wt, L0_attn_Wx, L0_attn_bh, L0_attn_Wa, L0_attn_ba,
           L0_ffn_w1, L0_ffn_conv_w, L0_ffn_conv_b, L0_ffn_w2, _dbg=False):
    x = np.ascontiguousarray(np.asarray(L0_attn_x, dtype=np.float32)).reshape(-1, D)
    Wt = np.asarray(L0_attn_Wt, dtype=np.float32)
    Wx = np.asarray(L0_attn_Wx, dtype=np.float32)
    bh = np.asarray(L0_attn_bh, dtype=np.float32)
    Wa = np.asarray(L0_attn_Wa, dtype=np.float32)
    ba = np.asarray(L0_attn_ba, dtype=np.float32)
    w1 = np.asarray(L0_ffn_w1, dtype=np.float32)
    cw = np.asarray(L0_ffn_conv_w, dtype=np.float32)
    cb = np.asarray(L0_ffn_conv_b, dtype=np.float32)
    w2 = np.asarray(L0_ffn_w2, dtype=np.float32)

    key = bool(_dbg)
    if key not in _NC_CACHE:
        _NC_CACHE[key] = build(dbg=_dbg)
    nc = _NC_CACHE[key]

    in_maps = [_core_inputs(c, x, Wt, Wx, bh, Wa, ba, w1, cw, cb, w2)
               for c in range(NCORES)]
    res = run_bass_kernel_spmd(nc, in_maps, core_ids=list(range(NCORES)))
    out = np.concatenate([res.results[c]["y"] for c in range(NCORES)], axis=0)
    if _dbg:
        return out.reshape(4, T_SEQ, D), res
    return out.reshape(4, T_SEQ, D)
